# version 01
# baseline (speedup 1.0000x reference)
import numpy as np
from contextlib import ExitStack

import concourse.bass as bass
import concourse.mybir as mybir
from concourse.bass_utils import run_bass_kernel_spmd
from concourse.alu_op_type import AluOpType as Op

F32 = mybir.dt.float32
BF16 = mybir.dt.bfloat16
AF = mybir.ActivationFunctionType
AX = mybir.AxisListType

D = 1024
I = 2816
TOK = 2048
NT = TOK // 128
S = 8192

N_PHASES = 3
FENCE_LVL = 2
SILU_U = True
DEBUG_DUMPS = False


def build_nc():
    nc = bass.Bass("TRN2", target_bir_lowering=False)

    x_d = nc.dram_tensor("x", [TOK, D], F32, kind="ExternalInput").ap()
    wg_d = nc.dram_tensor("wg", [I, D], F32, kind="ExternalInput").ap()
    wu_d = nc.dram_tensor("wu", [I, D], F32, kind="ExternalInput").ap()
    wd_d = nc.dram_tensor("wd", [D, I], F32, kind="ExternalInput").ap()
    wq_d = nc.dram_tensor("wqo", [D, 256], F32, kind="ExternalInput").ap()
    wk_d = nc.dram_tensor("wko", [D, 256], F32, kind="ExternalInput").ap()
    wv_d = nc.dram_tensor("wvo", [D, 256], F32, kind="ExternalInput").ap()
    wo_d = nc.dram_tensor("woo", [256, D], F32, kind="ExternalInput").ap()
    bq_d = nc.dram_tensor("bqo", [1, 256], F32, kind="ExternalInput").ap()
    bk_d = nc.dram_tensor("bko", [1, 256], F32, kind="ExternalInput").ap()
    bv_d = nc.dram_tensor("bvo", [1, 256], F32, kind="ExternalInput").ap()
    bo_d = nc.dram_tensor("bo", [1, D], F32, kind="ExternalInput").ap()
    out_d = nc.dram_tensor("out", [TOK, D], F32, kind="ExternalOutput").ap()

    wgq_s = nc.dram_tensor("wgq_s", [I, D], BF16).ap()
    wuq_s = nc.dram_tensor("wuq_s", [I, D], BF16).ap()
    wdq_s = nc.dram_tensor("wdq_s", [D, I], BF16).ap()
    xq_s = nc.dram_tensor("xq_s", [TOK, D], BF16).ap()
    hq_s = nc.dram_tensor("hq_s", [TOK, I], BF16).ap()
    ag1_in = nc.dram_tensor("ag1_in", [TOK, D], BF16).ap()
    dbg = "ExternalOutput" if DEBUG_DUMPS else "Internal"
    ag1_out = nc.dram_tensor("ag1_out", [S, D], BF16).ap()
    qt_s = nc.dram_tensor("qt_s", [2, 128, S], BF16, kind=dbg).ap()
    kt_s = nc.dram_tensor("kt_s", [2, 128, S], BF16, kind=dbg).ap()
    v_s = nc.dram_tensor("v_s", [S, 256], BF16, kind=dbg).ap()
    attnT_s = nc.dram_tensor("attnT_s", [2, 128, S], BF16, kind=dbg).ap()
    rs_in = nc.dram_tensor("rs_in", [S, D], BF16).ap()
    rs_out = nc.dram_tensor("rs_out", [TOK, D], BF16).ap()
    if DEBUG_DUMPS:
        ag1_dbg = nc.dram_tensor("ag1_dbg", [S, D], BF16, kind="ExternalOutput").ap()
        rso_dbg = nc.dram_tensor("rso_dbg", [TOK, D], BF16, kind="ExternalOutput").ap()
        rsi_dbg = nc.dram_tensor("rsi_dbg", [S, D], BF16, kind="ExternalOutput").ap()
        mc_dbg = nc.dram_tensor("mc_dbg", [2, 16, 128], mybir.dt.float32, kind="ExternalOutput").ap()
        pt_dbg = nc.dram_tensor("pt_dbg", [8, 128, 1024], BF16, kind="ExternalOutput").ap()
        rlb_dbg = nc.dram_tensor("rlb_dbg", [4, 16, 512], mybir.dt.float32, kind="ExternalOutput").ap()

    groups = [[0, 1, 2, 3], [4, 5, 6, 7]]

    es = ExitStack()
    plan = []
    cnt = {}
    sems = {}

    def mksem(name):
        sems[name] = es.enter_context(nc.semaphore(name))
        cnt[name] = 0

    def emit(eng, fn):
        plan.append((eng, fn))

    def wait(eng, sem, val=None):
        v = cnt[sem] if val is None else val
        if v > 0:
            emit(eng, lambda q, s=sems[sem], v=v: q.wait_ge(s, v))

    def bump(sem, k):
        cnt[sem] += k
        return sems[sem], cnt[sem]

    def op(eng, fn, sem=None, k=1):
        if sem is None:
            emit(eng, lambda q, f=fn: f(q))
        else:
            s, _ = bump(sem, k)
            emit(eng, lambda q, f=fn, s=s, k=k: f(q).then_inc(s, k))

    def dma(eng, dst, src, sem, transpose=False):
        s, _ = bump(sem, 16)
        if transpose:
            emit(eng, lambda q, d=dst, i=src, s=s: q.dma_start_transpose(d, i).then_inc(s, 16))
        else:
            emit(eng, lambda q, d=dst, i=src, s=s: q.dma_start(out=d, in_=i).then_inc(s, 16))

    with es:
        for s in ["ld", "gld", "st", "te", "dv", "ac", "cc1", "cc2"]:
            mksem(s)

        sb = lambda n, sh, dt=BF16: es.enter_context(nc.sbuf_tensor(n, list(sh), dt))

        R = sb("R", (128, NT * 1024), F32)
        mask = sb("mask", (128, 896), BF16)
        ones1 = sb("ones1", (1, 128), BF16)
        ones1f = sb("ones1f", (1, 128), F32)
        sf = sb("sf", (128, 16), F32)
        identf = sb("identf", (128, 128), F32)
        xcb = sb("xcb", (128, 1024), F32)
        sqdump = sb("sqdump", (128, 1024), F32)
        xnbf = sb("xnbf", (128, 1024), BF16)
        xnf = sb("xnf", (128, 1024), F32)
        boq = sb("boq", (1, D), BF16)
        bqt = sb("bqt", (128, 2), F32)
        bkt = sb("bkt", (128, 2), F32)
        bvr = sb("bvr", (1, 256), BF16)
        rln = sb("rln", (1, 512), F32)
        rlb = sb("rlb", (64, 512), F32)
        oT_st = sb("oT_st", (64, 512), BF16)
        mbf = sb("mbf", (1, 1), BF16)

        eps1 = sf[:, 0:1]
        lsum = sf[:, 1:2]; lmean = sf[:, 2:3]; lvar = sf[:, 3:4]
        llnt = sf[:, 4:5]; lrstd = sf[:, 5:6]
        cmcol = sf[:, 6:7]; mcol = sf[:, 8:9]
        msb = sf[0:1, 9:10]

        AB = sb("AB", (128, 37888), BF16)
        AB2 = sb("AB2", (128, 12480), BF16)

        wstage = AB[:, 0:5632].bitcast(F32)
        xqT = AB[:, 0:16384]
        wgT = [AB[:, 16384:20480], AB[:, 20480:24576]]
        wuT = [AB[:, 24576:28672], AB[:, 28672:32768]]
        gbuf = [AB[:, 32768:33792].bitcast(F32), AB[:, 33792:34816].bitcast(F32)]
        hbuf = [AB[:, 34816:35840].bitcast(F32), AB[:, 35840:36864].bitcast(F32)]
        hqtile = [AB[:, 36864:37376], AB[:, 37376:37888]]
        hqT = AB[:, 0:22528]
        wdT = AB[:, 22528:33792]
        ktb = [AB[:, 0:8192], AB[:, 8192:16384]]
        qtb = [AB[:, 16384:16896], AB[:, 16896:17408]]
        VAb = [AB[:, 17408:25728], AB[:, 25728:34048]]
        pjw = AB[:, 0:6144]
        wq_sb = AB[:, 0:2048]; wk_sb = AB[:, 2048:4096]; wv_sb = AB[:, 4096:6144]
        pjstage = AB[:, 6144:6656]
        wo_sb = AB[:, 0:2048]

        qa1 = AB2[:, 0:2048].bitcast(F32)
        qn6 = AB2[:, 2048:4096].bitcast(F32)
        qga = AB2[:, 4096:5120]
        qgb = AB2[:, 5120:6144]
        qhb = AB2[:, 6144:7168]
        qq1 = AB2[:, 7168:8192]
        qq2 = AB2[:, 8192:9216]
        qout = AB2[:, 9216:10240]
        qbm = AB2[:, 10240:10368].bitcast(F32)
        qsc = AB2[:, 10368:10496].bitcast(F32)
        qrc = AB2[:, 10496:10624].bitcast(F32)
        iotf = AB2[:, 0:1792].bitcast(F32)
        xn2Tb = [AB2[:, 0:4096], AB2[:, 4096:8192]]
        PTr = [AB2[:, 8192:9216], AB2[:, 9216:10240], AB2[:, 10240:11264]]
        lhsT_pw = [AB2[:, 0:1024], AB2[:, 1024:2048], AB2[:, 2048:3072], AB2[:, 3072:4096]]
        rsbuf = [AB2[:, 4096:5120], AB2[:, 5120:6144]]

        PS = es.enter_context(nc.psum_tensor("PS", [128, 4096], F32))
        bank = lambda b: PS[:, 512 * b:512 * (b + 1)]

        def te_fence(scrap, n=512):
            op("te", lambda q, o=scrap[:, 0:n], r=mask[0:1, 0:n]:
               q.matmul(o, ones1[:], r, start=True, stop=True), sem="te")
            return cnt["te"]

        def quant_chain(src, width, dst):
            nb = width // 16
            a1 = qa1[:, 0:width]; n6 = qn6[:, 0:width]
            ga = qga[:, 0:width]; gb = qgb[:, 0:width]; hb = qhb[:, 0:width]
            q1 = qq1[:, 0:width]; q2 = qq2[:, 0:width]
            bm = qbm[:, 0:nb]; sc = qsc[:, 0:nb]; rc = qrc[:, 0:nb]
            op("dv", lambda q, s=src, a=a1: q.scalar_tensor_tensor(a, s, -1.0, s, Op.mult, Op.max))
            op("dv", lambda q, a=a1, b=bm, n=nb: q.tensor_reduce(
                b, a.rearrange("p (b i) -> p b i", b=n), AX.X, Op.max), sem="dv")
            wait("dv", "dv")
            op("dv", lambda q, b=bm, s=sc: q.tensor_scalar(s, b, 1.0 / 6.0, 1e-12, Op.mult, Op.max), sem="dv")
            wait("dv", "dv")
            op("dv", lambda q, s=sc, r=rc: q.reciprocal(r, s), sem="dv")
            wait("dv", "dv")
            op("dv", lambda q, a=a1, r=rc, n=nb, o=n6: q.tensor_tensor(
                o.rearrange("p (b i) -> p b i", b=n),
                a.rearrange("p (b i) -> p b i", b=n),
                r[:, :, None].broadcast_to([128, n, 16]), Op.mult))
            op("dv", lambda q, n=n6, o=ga: q.tensor_scalar(o, n, 0.25, None, Op.is_gt))
            op("dv", lambda q, n=n6, i=ga, o=gb: q.scalar_tensor_tensor(o, n, 0.75, i, Op.is_gt, Op.add))
            op("dv", lambda q, n=n6, i=gb, o=ga: q.scalar_tensor_tensor(o, n, 1.25, i, Op.is_gt, Op.add))
            op("dv", lambda q, n=n6, i=ga, o=gb: q.scalar_tensor_tensor(o, n, 1.75, i, Op.is_gt, Op.add))
            op("dv", lambda q, n=n6, o=ga: q.tensor_scalar(o, n, 2.5, None, Op.is_gt))
            op("dv", lambda q, n=n6, i=ga, o=hb: q.scalar_tensor_tensor(o, n, 3.5, i, Op.is_gt, Op.add))
            op("dv", lambda q, n=n6, i=hb, o=ga: q.scalar_tensor_tensor(o, n, 5.0, i, Op.is_gt, Op.add))
            op("dv", lambda q, n=n6, i=ga, o=hb: q.scalar_tensor_tensor(o, n, 5.0, i, Op.is_gt, Op.add))
            op("dv", lambda q, g=gb, h=hb, o=q1: q.scalar_tensor_tensor(o, g, 0.5, h, Op.mult, Op.add))
            op("dv", lambda q, a=q1, r=sc, n=nb, o=q2: q.tensor_tensor(
                o.rearrange("p (b i) -> p b i", b=n),
                a.rearrange("p (b i) -> p b i", b=n),
                r[:, :, None].broadcast_to([128, n, 16]), Op.mult))
            op("dv", lambda q, s=src, m=q2, o=q1: q.scalar_tensor_tensor(o, s, 0.0, m, Op.is_gt, Op.mult))
            op("dv", lambda q, a=q1, m=q2, o=dst: q.scalar_tensor_tensor(o, a, 2.0, m, Op.mult, Op.subtract),
               sem="dv")

        def emit_ln(rt, dst):
            wait("dv", "ac")
            op("dv", lambda q, r=rt: q.tensor_reduce(lsum, r, AX.X, Op.add), sem="dv")
            wait("dv", "dv")
            op("dv", lambda q: q.tensor_scalar(lmean, lsum, 1.0 / D, None, Op.mult), sem="dv")
            wait("dv", "dv")
            op("dv", lambda q, r=rt: q.tensor_scalar(xcb[:], r, lmean, None, Op.subtract), sem="dv")
            wait("ac", "dv")
            op("ac", lambda q: q.activation(sqdump[:], xcb[:], AF.Square, accum_out=lvar), sem="ac")
            wait("ac", "ac")
            op("ac", lambda q: q.activation(llnt, lvar, AF.Ln, bias=eps1, scale=1.0 / D), sem="ac")
            wait("ac", "ac")
            op("ac", lambda q: q.activation(lrstd, llnt, AF.Exp, scale=-0.5), sem="ac")
            wait("dv", "ac")
            op("dv", lambda q, o=dst: q.tensor_scalar(o, xcb[:], lrstd, None, Op.mult))

        for i in range(4):
            dma("sync", R[:, i * 4096:(i + 1) * 4096].rearrange("p (t d) -> p t d", d=1024),
                x_d.rearrange("(t p) d -> p t d", p=128)[:, i * 4:(i + 1) * 4, :], "ld")
        op("dv", lambda q: q.memset(eps1, 1e-5))
        op("dv", lambda q: q.memset(ones1[:], 1.0))
        op("dv", lambda q: q.memset(ones1f[:], 1.0))
        op("gpsimd", lambda q: q.iota(iotf[:], [[1, 896]], base=-384, channel_multiplier=-1,
                                      allow_small_or_imprecise_dtypes=True), sem="gld", k=1)
        wait("dv", "gld")
        op("dv", lambda q: q.tensor_scalar(mask[:], iotf[:], 0.0, None, Op.is_ge), sem="dv")
        op("gpsimd", lambda q: q.iota(identf[:], [[1, 128]], base=0, channel_multiplier=-1,
                                      allow_small_or_imprecise_dtypes=True), sem="gld", k=1)
        wait("dv", "gld")
        op("dv", lambda q: q.tensor_scalar(identf[:], identf[:], 0.0, None, Op.is_equal), sem="dv")
        for t, d in ((bqt, bq_d), (bkt, bk_d)):
            for ntile in range(2):
                dma("sync", t[:, ntile:ntile + 1],
                    d.rearrange("o (n p) -> p (o n)", p=128)[:, ntile:ntile + 1], "ld")
        s_, _ = bump("gld", 16)
        emit("gpsimd", lambda q, s=s_: q.dma_start(out=bvr[:], in_=bv_d[:]).then_inc(s, 16))
        dma("sync", xcb[0:1, :], bo_d[:], "ld")
        wait("dv", "ld")
        op("dv", lambda q: q.tensor_scalar(boq[:], xcb[0:1, :], 0.25, None, Op.mult), sem="dv")

        def wprep(src_d, dst_s, rows, width):
            subs = [(o, min(1024, width - o)) for o in range(0, width, 1024)]
            for t in range(rows // 128):
                wt = wstage[:, 0:width]
                wait("sync", "dv")
                dma("sync", wt, src_d[t * 128:(t + 1) * 128, :], "ld")
                wait("dv", "ld")
                for (o, sw) in subs:
                    wait("dv", "st")
                    quant_chain(wt[:, o:o + sw], sw, qout[:, 0:sw])
                    wait("sync", "dv")
                    dma("sync", dst_s[t * 128:(t + 1) * 128, o:o + sw], qout[:, 0:sw], "st")

        wprep(wg_d, wgq_s, I, D)
        wprep(wu_d, wuq_s, I, D)
        wprep(wd_d, wdq_s, D, I)
        wq_done_st = cnt["st"]

        def emit_ffn():
            for i in range(NT):
                rt = R[:, i * 1024:(i + 1) * 1024]
                emit_ln(rt, xnf[:])
                wait("dv", "st")
                quant_chain(xnf[:], 1024, qout)
                wait("sync", "dv")
                dma("sync", xq_s[i * 128:(i + 1) * 128, :], qout, "st")
            wait("sync", "st")
            for kc in range(8):
                dma("sync", xqT[:, kc * 2048:(kc + 1) * 2048],
                    xq_s[:, kc * 128:(kc + 1) * 128], "ld", transpose=True)

            slabs = [(0, 512), (512, 512), (1024, 512), (1536, 512), (2048, 512), (2560, 256)]
            wld_marks = []

            def load_wslab(s):
                i0, w = slabs[s]
                par = s % 2
                wait("sync", "te")
                for kc in range(8):
                    dma("sync", wgT[par][:, kc * 512:kc * 512 + w],
                        wgq_s[i0:i0 + w, kc * 128:(kc + 1) * 128], "ld", transpose=True)
                    dma("sync", wuT[par][:, kc * 512:kc * 512 + w],
                        wuq_s[i0:i0 + w, kc * 128:(kc + 1) * 128], "ld", transpose=True)
                wld_marks.append(cnt["ld"])

            load_wslab(0)
            load_wslab(1)
            it = 0
            free_g = {}
            free_u = {}
            hq_st = {}
            for s in range(6):
                i0, w = slabs[s]
                par = s % 2
                wait("te", "ld", wld_marks[s])
                for tt in range(NT):
                    p = it % 2
                    gbank = bank(0 + p * 2)
                    ubank = bank(1 + p * 2)
                    if p in free_g:
                        wait("te", "ac", free_g[p])
                    if p in free_u:
                        wait("te", "dv", free_u[p])
                    for kc in range(8):
                        op("te", lambda q, o=gbank[:, 0:w],
                           l=xqT[:, kc * 2048 + tt * 128: kc * 2048 + tt * 128 + 128],
                           r=wgT[par][:, kc * 512:kc * 512 + w], st=(kc == 0), sp=(kc == 7):
                           q.matmul(o, l, r, start=st, stop=sp),
                           sem="te" if kc == 7 else None)
                    g_done = cnt["te"]
                    for kc in range(8):
                        op("te", lambda q, o=ubank[:, 0:w],
                           l=xqT[:, kc * 2048 + tt * 128: kc * 2048 + tt * 128 + 128],
                           r=wuT[par][:, kc * 512:kc * 512 + w], st=(kc == 0), sp=(kc == 7):
                           q.matmul(o, l, r, start=st, stop=sp),
                           sem="te" if kc == 7 else None)
                    u_done = cnt["te"]
                    wait("ac", "te", u_done if SILU_U else g_done)
                    wait("ac", "dv", free_u.get(p, 0))
                    op("ac", lambda q, i_=gbank[:, 0:w], o=gbuf[p][:, 0:w]:
                       q.activation(o, i_, AF.Silu), sem="ac")
                    free_g[p] = cnt["ac"]
                    wait("dv", "ac")
                    wait("dv", "te", u_done)
                    wait("dv", "st", hq_st.get(p, 0))
                    op("dv", lambda q, a=gbuf[p][:, 0:w], b=ubank[:, 0:w], o=hbuf[p][:, 0:w]:
                       q.tensor_tensor(o, a, b, Op.mult), sem="dv")
                    free_u[p] = cnt["dv"]
                    wait("dv", "st")
                    quant_chain(hbuf[p][:, 0:w], w, hqtile[p][:, 0:w])
                    wait("sync", "dv")
                    dma("sync", hq_s[tt * 128:(tt + 1) * 128, i0:i0 + w], hqtile[p][:, 0:w], "st")
                    hq_st[p] = cnt["st"]
                    it += 1
                if s + 2 < 6:
                    load_wslab(s + 2)
            wait("sync", "st")
            wait("te", "ac")
            wait("te", "dv")
            free_d = {}
            for th in range(2):
                for ic in range(22):
                    dma("sync", hqT[:, ic * 1024:(ic + 1) * 1024],
                        hq_s[th * 1024:(th + 1) * 1024, ic * 128:(ic + 1) * 128],
                        "ld", transpose=True)
                hq_ld = cnt["ld"]
                for nh in range(2):
                    for ic in range(22):
                        dma("sync", wdT[:, ic * 512:(ic + 1) * 512],
                            wdq_s[nh * 512:(nh + 1) * 512, ic * 128:(ic + 1) * 128],
                            "ld", transpose=True)
                    wd_ld = cnt["ld"]
                    wait("te", "ld", hq_ld)
                    wait("te", "ld", wd_ld)
                    for t8 in range(8):
                        tt = th * 8 + t8
                        p = t8 % 2
                        dbank = bank(p)
                        if p in free_d:
                            wait("te", "dv", free_d[p])
                        for ic in range(22):
                            op("te", lambda q, o=dbank,
                               l=hqT[:, ic * 1024 + t8 * 128: ic * 1024 + t8 * 128 + 128],
                               r=wdT[:, ic * 512:(ic + 1) * 512], st=(ic == 0), sp=(ic == 21):
                               q.matmul(o, l, r, start=st, stop=sp),
                               sem="te" if ic == 21 else None)
                        wait("dv", "te", cnt["te"] + (1 if FENCE_LVL >= 1 else 0))
                        rsl = R[:, tt * 1024 + nh * 512: tt * 1024 + nh * 512 + 512]
                        op("dv", lambda q, b=dbank, r=rsl:
                           q.scalar_tensor_tensor(r, b, 0.5, r, Op.mult, Op.add), sem="dv")
                        free_d[p] = cnt["dv"]
                    wait("sync", "te")
                wait("sync", "te")
            if FENCE_LVL >= 1:
                te_fence(bank(2))

        emit_ffn()

        if N_PHASES >= 2:
            for i in range(NT):
                rt = R[:, i * 1024:(i + 1) * 1024]
                wait("dv", "st")
                emit_ln(rt, xnbf[:])
                op("dv", lambda q: q.tensor_copy(sf[:, 10:11], xnbf[:, 0:1]), sem="dv")
                wait("sync", "dv")
                dma("sync", ag1_in[i * 128:(i + 1) * 128, :], xnbf[:], "st")
            ag_ready = cnt["st"]
            wait("gpsimd", "te")
            wait("gpsimd", "dv")
            for wsb, wdr in ((wq_sb, wq_d), (wk_sb, wk_d), (wv_sb, wv_d)):
                s_, _ = bump("gld", 16)
                emit("gpsimd", lambda q, w=wsb, dd=wdr, s=s_: q.dma_start(
                    out=w.rearrange("p (kc n) -> p kc n", n=256),
                    in_=dd.rearrange("(kc p) n -> p kc n", p=128)).then_inc(s, 16))
            wait("gpsimd", "st", ag_ready)
            s_, _ = bump("cc1", 1)
            emit("gpsimd", lambda q, s=s_: q.collective_compute(
                "AllGather", mybir.AluOpType.bypass, replica_groups=groups,
                ins=[ag1_in.opt()], outs=[ag1_out.opt()]).then_inc(s, 1))

            wait("sync", "cc1", 1)
            wait("te", "gld")
            x_marks = []

            def load_xslice(ts):
                par = ts % 2
                wait("sync", "te")
                for kc in range(8):
                    dma("sync", xn2Tb[par][:, kc * 512:(kc + 1) * 512],
                        ag1_out[ts * 512:(ts + 1) * 512, kc * 128:(kc + 1) * 128],
                        "ld", transpose=True)
                x_marks.append(cnt["ld"])

            load_xslice(0)
            load_xslice(1)
            pj_free = {}
            st_stage = 0
            for ts in range(16):
                par = ts % 2
                wait("te", "ld", x_marks[ts])
                for pi, (wsb, dst_s, bt) in enumerate(
                        ((wq_sb, qt_s, bqt), (wk_sb, kt_s, bkt))):
                    for ntile in range(2):
                        bi = (pi * 2 + ntile) % 4
                        pb = bank(bi)
                        if bi in pj_free:
                            wait("te", "dv", pj_free[bi])
                        for kc in range(8):
                            op("te", lambda q, o=pb,
                               l=wsb[:, kc * 256 + ntile * 128: kc * 256 + ntile * 128 + 128],
                               r=xn2Tb[par][:, kc * 512:(kc + 1) * 512], st=(kc == 0), sp=(kc == 7):
                               q.matmul(o, l, r, start=st, stop=sp),
                               sem="te" if kc == 7 else None)
                        wait("dv", "te", cnt["te"] + 1)
                        wait("dv", "st", st_stage)
                        op("dv", lambda q, i_=pb, b=bt[:, ntile:ntile + 1]:
                           q.tensor_scalar(pjstage[:], i_, b, None, Op.add), sem="dv")
                        pj_free[bi] = cnt["dv"]
                        wait("sync", "dv")
                        dma("sync", dst_s[ntile, :, ts * 512:(ts + 1) * 512], pjstage[:], "st")
                        st_stage = cnt["st"]
                for vt in range(4):
                    bi = 4 + (vt % 2)
                    pb = bank(bi)
                    if bi in pj_free:
                        wait("te", "dv", pj_free[bi])
                    for kc in range(8):
                        op("te", lambda q, o=pb[:, 0:256],
                           l=xn2Tb[par][:, kc * 512 + vt * 128: kc * 512 + vt * 128 + 128],
                           r=wv_sb[:, kc * 256:(kc + 1) * 256], st=(kc == 0):
                           q.matmul(o, l, r, start=st, stop=False))
                    op("te", lambda q, o=pb[:, 0:256]: q.matmul(
                        o, ones1[:], bvr[:], start=False, stop=True), sem="te")
                    wait("dv", "te", cnt["te"] + 1)
                    wait("dv", "st", st_stage)
                    op("dv", lambda q, i_=pb[:, 0:256]:
                       q.tensor_copy(pjstage[:, 0:256], i_), sem="dv")
                    pj_free[bi] = cnt["dv"]
                    wait("sync", "dv")
                    dma("sync", v_s[(ts * 4 + vt) * 128:(ts * 4 + vt) * 128 + 128, :],
                        pjstage[:, 0:256], "st")
                    st_stage = cnt["st"]
                if ts + 2 < 16:
                    load_xslice(ts + 2)
            te_fence(bank(6))
            proj_done_st = cnt["st"]

            p_t = [PS[:, 0:1024], PS[:, 1024:2048]]
            av_bank = [[bank(4), bank(5)], [bank(6), bank(7)]]
            av_free = {}
            par_free = {}
            ptr_free = {}

            def load_ktq(idx):
                hp, g = divmod(idx, 16)
                par = idx % 2
                wait("sync", "te")
                wait("sync", "st")
                n = 512 * (g + 1)
                dma("sync", ktb[par][:, 0:n], kt_s[hp, :, 0:n], "ld")
                dma("sync", qtb[par][:, :], qt_s[hp, :, g * 512:(g + 1) * 512], "ld")
                return cnt["ld"]

            wait("sync", "st", proj_done_st)
            wait("sync", "te")
            for h2 in range(2):
                dma("sync", VAb[0].rearrange("p (h t o) -> p h t o", h=2, t=64)[:, h2, :, 0:64],
                    v_s.rearrange("(t p) (hp h d) -> p hp h t d", p=128, hp=2, h=2)[:, 0, h2, :, :],
                    "ld")
            op("dv", lambda q: q.memset(
                VAb[0].rearrange("p (h t o) -> p h t o", h=2, t=64)[:, :, :, 64:65], 1.0),
               sem="dv")
            va_ready = [cnt["dv"], None]
            ktq_marks = [load_ktq(0), load_ktq(1)]
            wait("te", "dv")
            for hp in range(2):
                if hp == 0:
                    for h2 in range(2):
                        dma("sync", VAb[1].rearrange("p (h t o) -> p h t o", h=2, t=64)[:, h2, :, 0:64],
                            v_s.rearrange("(t p) (hp h d) -> p hp h t d", p=128, hp=2, h=2)[:, 1, h2, :, :],
                            "ld")
                    op("dv", lambda q: q.memset(
                        VAb[1].rearrange("p (h t o) -> p h t o", h=2, t=64)[:, :, :, 64:65], 1.0),
                       sem="dv")
                    va_ready[1] = cnt["dv"]
                for g in range(16):
                    idx = hp * 16 + g
                    kpar = idx % 2
                    nsc = 4 * g + 4
                    order = list(range(4 * g, 4 * g + 4)) + list(range(0, 4 * g))
                    avset = g % 2
                    wait("te", "ld", ktq_marks[idx])
                    wait("te", "dv", va_ready[hp])
                    if avset in av_free:
                        wait("te", "dv", av_free[avset])
                    mc_done = None
                    dance_dv = None
                    st_marks = []

                    def emit_pair(si):
                        kc = order[si]
                        par = si % 2
                        pt = p_t[par]
                        if par in par_free:
                            wait("te", "ac", par_free[par])
                        if si == 2 and dance_dv is not None:
                            wait("te", "dv", dance_dv)
                        op("te", lambda q, o=pt[:, 0:512], l=ktb[kpar][0:64, kc * 128:(kc + 1) * 128],
                           r=qtb[kpar][0:64, :]:
                           q.matmul(o, l, r, start=True, stop=True))
                        op("te", lambda q, o=pt[:, 512:1024], l=ktb[kpar][64:128, kc * 128:(kc + 1) * 128],
                           r=qtb[kpar][64:128, :]:
                           q.matmul(o, l, r, start=True, stop=True), sem="te")
                        st_marks.append(cnt["te"])

                    def emit_consumer(si, fence_val):
                        kc = order[si]
                        par = si % 2
                        slot = si % 3
                        pt = p_t[par]
                        w = fence_val
                        if slot in ptr_free:
                            w = max(w, ptr_free[slot])
                        wait("ac", "te", w)
                        if si == 0:
                            wait("ac", "dv", mc_done)
                        op("ac", lambda q, i_=pt, o=PTr[slot]: q.activation(
                            o, i_, AF.Exp, bias=mcol, scale=0.125), sem="ac")
                        par_free[par] = cnt["ac"]
                        exp_done = cnt["ac"]
                        gate_sem, gate_val = "ac", exp_done
                        if kc >= 4 * g:
                            moff = 384 - 128 * (kc - 4 * g)
                            wait("dv", "ac", exp_done)
                            op("dv", lambda q, o=PTr[slot], mo=moff: q.tensor_tensor(
                                o.rearrange("p (e f) -> p e f", e=2),
                                o.rearrange("p (e f) -> p e f", e=2),
                                mask[:, mo:mo + 512][:, None, :].broadcast_to([128, 2, 512]),
                                Op.mult), sem="dv")
                            gate_sem, gate_val = "dv", cnt["dv"]
                        if DEBUG_DUMPS and hp == 0 and g == 1:
                            wait("sync", gate_sem, gate_val)
                            dma("sync", pt_dbg[si, :, :], PTr[slot][:], "st")
                            gate_sem, gate_val = "st", cnt["st"]
                        wait("te", gate_sem, gate_val)
                        for e in range(2):
                            op("te", lambda q, o=av_bank[avset][e][0:65, :],
                               l=VAb[hp][:, e * 4160 + kc * 65: e * 4160 + kc * 65 + 65],
                               r=PTr[slot][:, e * 512:(e + 1) * 512],
                               st=(si == 0), sp=(si == nsc - 1):
                               q.matmul(o, l, r, start=st, stop=sp),
                               sem="te" if e == 1 else None)
                        ptr_free[slot] = cnt["te"]

                    for si in range(nsc):
                        emit_pair(si)
                        if si == 1:
                            avB0 = av_bank[avset][0]
                            avB1 = av_bank[avset][1]
                            wait("dv", "te", st_marks[1])
                            wait("dv", "ac")
                            op("dv", lambda q: q.tensor_reduce(
                                cmcol, p_t[0], AX.X, Op.max), sem="dv")
                            dance_dv = cnt["dv"]
                            wait("te", "dv", dance_dv)
                            op("te", lambda q, o=avB0: q.transpose(
                                o[0:1, 0:128], cmcol, identf[:]), sem="te")
                            f1 = te_fence(avB1)
                            wait("dv", "te", f1)
                            op("dv", lambda q, a=avB0: q.tensor_reduce(
                                msb, a[0:1, 0:128], AX.X, Op.max), sem="dv")
                            wait("dv", "dv")
                            op("dv", lambda q: q.tensor_copy(mbf[:], msb), sem="dv")
                            wait("te", "dv")
                            op("te", lambda q, o=avB0: q.matmul(
                                o[:, 0:1], ones1[:], mbf[:], start=True, stop=True), sem="te")
                            f2 = te_fence(avB1)
                            wait("dv", "te", f2)
                            op("dv", lambda q, a=avB0: q.tensor_scalar(
                                mcol, a[:, 0:1], -0.125, None, Op.mult), sem="dv")
                            mc_done = cnt["dv"]
                            if DEBUG_DUMPS:
                                wait("sync", "dv", mc_done)
                                dma("sync", mc_dbg.rearrange("h g p -> (h g) p")[
                                    hp * 16 + g: hp * 16 + g + 1, :].rearrange("o p -> p o"),
                                    mcol, "st")
                            emit_consumer(0, st_marks[1])
                        elif si >= 2:
                            emit_consumer(si - 1, st_marks[si])
                    wait("te", "ac", par_free.get(nsc % 2, 0))
                    f3 = te_fence(p_t[nsc % 2][:, 0:512])
                    emit_consumer(nsc - 1, f3)
                    wait("te", "ac", par_free.get((nsc - 1) % 2, 0))
                    f4 = te_fence(p_t[(nsc - 1) % 2][:, 0:512])
                    for e in range(2):
                        ab = av_bank[avset][e]
                        spare = av_bank[1 - avset][e]
                        wait("ac", "te", f4)
                        op("ac", lambda q, a=ab: q.activation(rln[:], a[64:65, :], AF.Ln), sem="ac")
                        wait("te", "ac")
                        op("te", lambda q, sp=spare: q.matmul(
                            sp[0:64, :], ones1f[0:1, 0:64], rln[:], start=True, stop=True),
                           sem="te")
                        f5 = te_fence(av_bank[1 - avset][1 - e])
                        wait("ac", "te", f5)
                        wait("ac", "dv")
                        op("ac", lambda q, sp=spare: q.activation(
                            rlb[:], sp[0:64, :], AF.Exp, scale=-1.0), sem="ac")
                        if DEBUG_DUMPS:
                            wait("sync", "ac")
                            dma("sync", rlb_dbg.rearrange("h g v -> (h g) v")[(2 * hp + e) * 16 + g:(2 * hp + e) * 16 + g + 1, :], rlb[0:1, :], "st")
                        wait("dv", "ac")
                        wait("dv", "st")
                        op("dv", lambda q, a=ab: q.tensor_tensor(
                            oT_st[:], a[0:64, :], rlb[:], Op.mult), sem="dv")
                        wait("sync", "dv")
                        dma("sync", attnT_s[hp, e * 64:(e + 1) * 64, g * 512:(g + 1) * 512],
                            oT_st[:], "st")
                    av_free[avset] = cnt["dv"]
                    wait("te", "ac")
                    if idx + 2 < 32:
                        ktq_marks.append(load_ktq(idx + 2))

            wait("gpsimd", "te")
            s_, _ = bump("gld", 16)
            emit("gpsimd", lambda q, s=s_: q.dma_start(
                out=wo_sb.rearrange("p (kc n) -> p kc n", n=1024),
                in_=wo_d.rearrange("(kc p) n -> p kc n", p=128)).then_inc(s, 16))
            wait("te", "gld")
            wait("te", "ac")
            wait("te", "dv")
            lw_marks = []

            def load_lhs(grp):
                par = grp % 2
                wait("sync", "te")
                for hp in range(2):
                    dma("sync", lhsT_pw[par * 2 + hp],
                        attnT_s[hp, :, grp * 1024:(grp + 1) * 1024], "ld")
                lw_marks.append(cnt["ld"])

            wait("sync", "st")
            load_lhs(0)
            load_lhs(1)
            pw_free = {}
            rsb_st = {}
            for grp in range(8):
                par = grp % 2
                wait("te", "ld", lw_marks[grp])
                for sub in range(8):
                    p = (grp * 8 + sub) % 2
                    pb2 = PS[:, p * 1024:(p + 1) * 1024]
                    if p in pw_free:
                        wait("te", "dv", pw_free[p])
                    for nhf in range(2):
                        o = pb2[:, nhf * 512:(nhf + 1) * 512]
                        for hp in range(2):
                            op("te", lambda q, o=o,
                               l=lhsT_pw[par * 2 + hp][:, sub * 128:(sub + 1) * 128],
                               r=wo_sb[:, hp * 1024 + nhf * 512: hp * 1024 + nhf * 512 + 512],
                               st=(hp == 0):
                               q.matmul(o, l, r, start=st, stop=False))
                        op("te", lambda q, o=o, r=boq[:, nhf * 512:(nhf + 1) * 512]:
                           q.matmul(o, ones1[:], r, start=False, stop=True), sem="te")
                    wait("dv", "te", cnt["te"] + 1)
                    wait("dv", "st", rsb_st.get(p, 0))
                    op("dv", lambda q, i_=pb2, o=rsbuf[p]: q.tensor_copy(o, i_), sem="dv")
                    pw_free[p] = cnt["dv"]
                    wait("sync", "dv")
                    tt = grp * 8 + sub
                    dma("sync", rs_in[tt * 128:(tt + 1) * 128, :], rsbuf[p], "st")
                    rsb_st[p] = cnt["st"]
                if grp + 2 < 8:
                    load_lhs(grp + 2)
            te_fence(bank(4))
            wait("gpsimd", "st")
            s_, _ = bump("cc2", 1)
            emit("gpsimd", lambda q, s=s_: q.collective_compute(
                "ReduceScatter", mybir.AluOpType.add, replica_groups=groups,
                ins=[rs_in.opt()], outs=[rs_out.opt()]).then_inc(s, 1))
            wait("sync", "cc2", 1)
            rsd = {}
            for i in range(NT):
                par = i % 2
                if par in rsd:
                    wait("sync", "dv", rsd[par])
                dma("sync", rsbuf[par], rs_out[i * 128:(i + 1) * 128, :], "ld")
                wait("dv", "ld")
                op("dv", lambda q, b=rsbuf[par], r=R[:, i * 1024:(i + 1) * 1024]:
                   q.scalar_tensor_tensor(r, b, 1.0, r, Op.mult, Op.add), sem="dv")
                rsd[par] = cnt["dv"]

        if N_PHASES >= 3:
            emit_ffn()

        if FENCE_LVL >= 2:
            te_fence(bank(4))
        if DEBUG_DUMPS and N_PHASES >= 2:
            dma("sync", ag1_dbg[:], ag1_out[:], "st")
            dma("sync", rso_dbg[:], rs_out[:], "st")
            dma("sync", rsi_dbg[:], rs_in[:], "st")
        wait("sync", "dv")
        for i in range(4):
            dma("sync", out_d.rearrange("(t p) d -> p t d", p=128)[:, i * 4:(i + 1) * 4, :],
                R[:, i * 4096:(i + 1) * 4096].rearrange("p (t d) -> p t d", d=1024), "st")
        wait("sync", "st")

        with nc.Block() as block:
            @block.sync
            def _(q):
                for e, fn in plan:
                    if e == "sync":
                        fn(q)

            @block.tensor
            def _(q):
                for e, fn in plan:
                    if e == "te":
                        fn(q)

            @block.vector
            def _(q):
                for e, fn in plan:
                    if e == "dv":
                        fn(q)

            @block.scalar
            def _(q):
                for e, fn in plan:
                    if e == "ac":
                        fn(q)

            @block.gpsimd
            def _(q):
                for e, fn in plan:
                    if e == "gpsimd":
                        fn(q)

    return nc


_NC_CACHE = None


def kernel(**inputs):
    global _NC_CACHE
    X = np.ascontiguousarray(np.asarray(inputs["L0_attn_inputs"], np.float32).reshape(16384, 1024))
    Wq = np.asarray(inputs["L0_attn_Wq"], np.float32)
    Wk = np.asarray(inputs["L0_attn_Wk"], np.float32)
    Wv = np.asarray(inputs["L0_attn_Wv"], np.float32)
    Wo = np.asarray(inputs["L0_attn_Wo"], np.float32)
    bq = np.asarray(inputs["L0_attn_bq"], np.float32)
    bk = np.asarray(inputs["L0_attn_bk"], np.float32)
    bv = np.asarray(inputs["L0_attn_bv"], np.float32)
    bo = np.asarray(inputs["L0_attn_bo"], np.float32)
    wg = np.ascontiguousarray(np.asarray(inputs["L0_ffn_w_gate"], np.float32))
    wu = np.ascontiguousarray(np.asarray(inputs["L0_ffn_w_up"], np.float32))
    wd = np.ascontiguousarray(np.asarray(inputs["L0_ffn_w_down"], np.float32))

    if _NC_CACHE is None:
        _NC_CACHE = build_nc()
    nc = _NC_CACHE

    in_maps = []
    for c in range(8):
        s = 256 * (c % 4)
        in_maps.append({
            "x": np.ascontiguousarray(X[c * TOK:(c + 1) * TOK]),
            "wg": wg, "wu": wu, "wd": wd,
            "wqo": np.ascontiguousarray(Wq[:, s:s + 256]),
            "wko": np.ascontiguousarray(Wk[:, s:s + 256]),
            "wvo": np.ascontiguousarray(Wv[:, s:s + 256]),
            "woo": np.ascontiguousarray(Wo[s:s + 256, :]),
            "bqo": np.ascontiguousarray(bq[s:s + 256].reshape(1, 256)),
            "bko": np.ascontiguousarray(bk[s:s + 256].reshape(1, 256)),
            "bvo": np.ascontiguousarray(bv[s:s + 256].reshape(1, 256)),
            "bo": np.ascontiguousarray(bo.reshape(1, 1024)),
        })
    res = run_bass_kernel_spmd(nc, in_maps, core_ids=list(range(8)))
    if DEBUG_DUMPS:
        global LAST_RES
        LAST_RES = res.results
    out = np.concatenate([np.asarray(res.results[c]["out"]) for c in range(8)], axis=0)
    return out.reshape(2, 8192, 1024)


# version 02
# speedup vs baseline: 1.0050x; 1.0050x over previous
import numpy as np
from contextlib import ExitStack

import concourse.bass as bass
import concourse.mybir as mybir
from concourse.bass_utils import run_bass_kernel_spmd
from concourse.alu_op_type import AluOpType as Op

F32 = mybir.dt.float32
BF16 = mybir.dt.bfloat16
AF = mybir.ActivationFunctionType
AX = mybir.AxisListType

D = 1024
I = 2816
TOK = 2048
NT = TOK // 128
S = 8192

N_PHASES = 3
FENCE_LVL = 2
SILU_U = True
DEBUG_DUMPS = False


def build_nc():
    nc = bass.Bass("TRN2", target_bir_lowering=False)

    x_d = nc.dram_tensor("x", [TOK, D], F32, kind="ExternalInput").ap()
    wg_d = nc.dram_tensor("wg", [I, D], F32, kind="ExternalInput").ap()
    wu_d = nc.dram_tensor("wu", [I, D], F32, kind="ExternalInput").ap()
    wd_d = nc.dram_tensor("wd", [D, I], F32, kind="ExternalInput").ap()
    wq_d = nc.dram_tensor("wqo", [D, 256], F32, kind="ExternalInput").ap()
    wk_d = nc.dram_tensor("wko", [D, 256], F32, kind="ExternalInput").ap()
    wv_d = nc.dram_tensor("wvo", [D, 256], F32, kind="ExternalInput").ap()
    wo_d = nc.dram_tensor("woo", [256, D], F32, kind="ExternalInput").ap()
    bq_d = nc.dram_tensor("bqo", [1, 256], F32, kind="ExternalInput").ap()
    bk_d = nc.dram_tensor("bko", [1, 256], F32, kind="ExternalInput").ap()
    bv_d = nc.dram_tensor("bvo", [1, 256], F32, kind="ExternalInput").ap()
    bo_d = nc.dram_tensor("bo", [1, D], F32, kind="ExternalInput").ap()
    out_d = nc.dram_tensor("out", [TOK, D], F32, kind="ExternalOutput").ap()

    wgq_s = nc.dram_tensor("wgq_s", [I, D], BF16).ap()
    wuq_s = nc.dram_tensor("wuq_s", [I, D], BF16).ap()
    wdq_s = nc.dram_tensor("wdq_s", [D, I], BF16).ap()
    xq_s = nc.dram_tensor("xq_s", [TOK, D], BF16).ap()
    hq_s = nc.dram_tensor("hq_s", [TOK, I], BF16).ap()
    ag1_in = nc.dram_tensor("ag1_in", [TOK, D], BF16).ap()
    dbg = "ExternalOutput" if DEBUG_DUMPS else "Internal"
    ag1_out = nc.dram_tensor("ag1_out", [S, D], BF16).ap()
    qt_s = nc.dram_tensor("qt_s", [2, 128, S], BF16, kind=dbg).ap()
    kt_s = nc.dram_tensor("kt_s", [2, 128, S], BF16, kind=dbg).ap()
    v_s = nc.dram_tensor("v_s", [S, 256], BF16, kind=dbg).ap()
    attnT_s = nc.dram_tensor("attnT_s", [2, 128, S], BF16, kind=dbg).ap()
    rs_in = nc.dram_tensor("rs_in", [S, D], BF16).ap()
    rs_out = nc.dram_tensor("rs_out", [TOK, D], BF16).ap()
    if DEBUG_DUMPS:
        ag1_dbg = nc.dram_tensor("ag1_dbg", [S, D], BF16, kind="ExternalOutput").ap()
        rso_dbg = nc.dram_tensor("rso_dbg", [TOK, D], BF16, kind="ExternalOutput").ap()
        rsi_dbg = nc.dram_tensor("rsi_dbg", [S, D], BF16, kind="ExternalOutput").ap()
        mc_dbg = nc.dram_tensor("mc_dbg", [2, 16, 128], mybir.dt.float32, kind="ExternalOutput").ap()
        pt_dbg = nc.dram_tensor("pt_dbg", [8, 128, 1024], BF16, kind="ExternalOutput").ap()
        rlb_dbg = nc.dram_tensor("rlb_dbg", [4, 16, 512], mybir.dt.float32, kind="ExternalOutput").ap()

    groups = [[0, 1, 2, 3], [4, 5, 6, 7]]

    es = ExitStack()
    plan = []
    cnt = {}
    sems = {}

    def mksem(name):
        sems[name] = es.enter_context(nc.semaphore(name))
        cnt[name] = 0

    def emit(eng, fn):
        plan.append((eng, fn))

    def wait(eng, sem, val=None):
        v = cnt[sem] if val is None else val
        if v > 0:
            emit(eng, lambda q, s=sems[sem], v=v: q.wait_ge(s, v))

    def bump(sem, k):
        cnt[sem] += k
        return sems[sem], cnt[sem]

    def op(eng, fn, sem=None, k=1):
        if sem is None:
            emit(eng, lambda q, f=fn: f(q))
        else:
            s, _ = bump(sem, k)
            emit(eng, lambda q, f=fn, s=s, k=k: f(q).then_inc(s, k))

    def dma(eng, dst, src, sem, transpose=False):
        s, _ = bump(sem, 16)
        if transpose:
            emit(eng, lambda q, d=dst, i=src, s=s: q.dma_start_transpose(d, i).then_inc(s, 16))
        else:
            emit(eng, lambda q, d=dst, i=src, s=s: q.dma_start(out=d, in_=i).then_inc(s, 16))

    with es:
        for s in ["ld", "gld", "st", "te", "dv", "ac", "cc1", "cc2"]:
            mksem(s)

        sb = lambda n, sh, dt=BF16: es.enter_context(nc.sbuf_tensor(n, list(sh), dt))

        R = sb("R", (128, NT * 1024), F32)
        mask = sb("mask", (128, 896), BF16)
        ones1 = sb("ones1", (1, 128), BF16)
        ones1f = sb("ones1f", (1, 128), F32)
        sf = sb("sf", (128, 16), F32)
        identf = sb("identf", (128, 128), F32)
        xcb = sb("xcb", (128, 1024), F32)
        sqdump = sb("sqdump", (128, 1024), F32)
        xnbf = sb("xnbf", (128, 1024), BF16)
        xnf = sb("xnf", (128, 1024), F32)
        boq = sb("boq", (1, D), BF16)
        bqt = sb("bqt", (128, 2), F32)
        bkt = sb("bkt", (128, 2), F32)
        bvr = sb("bvr", (1, 256), BF16)
        rln = sb("rln", (1, 512), F32)
        rlb = sb("rlb", (64, 512), F32)
        oT_st = sb("oT_st", (64, 512), BF16)
        mbf = sb("mbf", (1, 1), BF16)

        eps1 = sf[:, 0:1]
        lsum = sf[:, 1:2]; lmean = sf[:, 2:3]; lvar = sf[:, 3:4]
        llnt = sf[:, 4:5]; lrstd = sf[:, 5:6]
        cmcol = sf[:, 6:7]; mcol = sf[:, 8:9]
        msb = sf[0:1, 9:10]

        AB = sb("AB", (128, 37888), BF16)
        AB2 = sb("AB2", (128, 12480), BF16)

        wstage = AB[:, 0:5632].bitcast(F32)
        wstage2 = AB[:, 5632:11264].bitcast(F32)
        xqT = AB[:, 0:16384]
        wgT = [AB[:, 16384:20480], AB[:, 20480:24576]]
        wuT = [AB[:, 24576:28672], AB[:, 28672:32768]]
        gbuf = [AB[:, 32768:33792].bitcast(F32), AB[:, 33792:34816].bitcast(F32)]
        hbuf = [AB[:, 34816:35840].bitcast(F32), AB[:, 35840:36864].bitcast(F32)]
        hqtile = [AB[:, 36864:37376], AB[:, 37376:37888]]
        hqT = AB[:, 0:22528]
        wdT = AB[:, 22528:33792]
        ktb = [AB[:, 0:8192], AB[:, 8192:16384]]
        qtb = [AB[:, 16384:16896], AB[:, 16896:17408]]
        VAb = [AB[:, 17408:25728], AB[:, 25728:34048]]
        pjw = AB[:, 0:6144]
        wq_sb = AB[:, 0:2048]; wk_sb = AB[:, 2048:4096]; wv_sb = AB[:, 4096:6144]
        pjstage = AB[:, 6144:6656]
        wo_sb = AB[:, 0:2048]

        qa1 = AB2[:, 0:2048].bitcast(F32)
        qn6 = AB2[:, 2048:4096].bitcast(F32)
        qga = AB2[:, 4096:5120]
        qgb = AB2[:, 5120:6144]
        qhb = AB2[:, 6144:7168]
        qq1 = AB2[:, 7168:8192]
        qq2 = AB2[:, 8192:9216]
        qout = AB2[:, 9216:10240]
        qout2 = AB2[:, 10624:11648]
        qbm = AB2[:, 10240:10368].bitcast(F32)
        qsc = AB2[:, 10368:10496].bitcast(F32)
        qrc = AB2[:, 10496:10624].bitcast(F32)
        iotf = AB2[:, 0:1792].bitcast(F32)
        xn2Tb = [AB2[:, 0:4096], AB2[:, 4096:8192]]
        PTr = [AB2[:, 8192:9216], AB2[:, 9216:10240], AB2[:, 10240:11264]]
        lhsT_pw = [AB2[:, 0:1024], AB2[:, 1024:2048], AB2[:, 2048:3072], AB2[:, 3072:4096]]
        rsbuf = [AB2[:, 4096:5120], AB2[:, 5120:6144]]

        PS = es.enter_context(nc.psum_tensor("PS", [128, 4096], F32))
        bank = lambda b: PS[:, 512 * b:512 * (b + 1)]

        def te_fence(scrap, n=512):
            op("te", lambda q, o=scrap[:, 0:n], r=mask[0:1, 0:n]:
               q.matmul(o, ones1[:], r, start=True, stop=True), sem="te")
            return cnt["te"]

        def quant_chain(src, width, dst):
            nb = width // 16
            a1 = qa1[:, 0:width]; n6 = qn6[:, 0:width]
            ga = qga[:, 0:width]; gb = qgb[:, 0:width]; hb = qhb[:, 0:width]
            q1 = qq1[:, 0:width]; q2 = qq2[:, 0:width]
            bm = qbm[:, 0:nb]; sc = qsc[:, 0:nb]; rc = qrc[:, 0:nb]
            op("dv", lambda q, s=src, a=a1: q.scalar_tensor_tensor(a, s, -1.0, s, Op.mult, Op.max))
            op("dv", lambda q, a=a1, b=bm, n=nb: q.tensor_reduce(
                b, a.rearrange("p (b i) -> p b i", b=n), AX.X, Op.max), sem="dv")
            wait("dv", "dv")
            op("dv", lambda q, b=bm, s=sc: q.tensor_scalar(s, b, 1.0 / 6.0, 1e-12, Op.mult, Op.max), sem="dv")
            wait("dv", "dv")
            op("dv", lambda q, s=sc, r=rc: q.reciprocal(r, s), sem="dv")
            wait("dv", "dv")
            op("dv", lambda q, a=a1, r=rc, n=nb, o=n6: q.tensor_tensor(
                o.rearrange("p (b i) -> p b i", b=n),
                a.rearrange("p (b i) -> p b i", b=n),
                r[:, :, None].broadcast_to([128, n, 16]), Op.mult))
            op("dv", lambda q, n=n6, o=ga: q.tensor_scalar(o, n, 0.25, None, Op.is_gt))
            op("dv", lambda q, n=n6, i=ga, o=gb: q.scalar_tensor_tensor(o, n, 0.75, i, Op.is_gt, Op.add))
            op("dv", lambda q, n=n6, i=gb, o=ga: q.scalar_tensor_tensor(o, n, 1.25, i, Op.is_gt, Op.add))
            op("dv", lambda q, n=n6, i=ga, o=gb: q.scalar_tensor_tensor(o, n, 1.75, i, Op.is_gt, Op.add))
            op("dv", lambda q, n=n6, o=ga: q.tensor_scalar(o, n, 2.5, None, Op.is_gt))
            op("dv", lambda q, n=n6, i=ga, o=hb: q.scalar_tensor_tensor(o, n, 3.5, i, Op.is_gt, Op.add))
            op("dv", lambda q, n=n6, i=hb, o=ga: q.scalar_tensor_tensor(o, n, 5.0, i, Op.is_gt, Op.add))
            op("dv", lambda q, n=n6, i=ga, o=hb: q.scalar_tensor_tensor(o, n, 5.0, i, Op.is_gt, Op.add))
            op("dv", lambda q, g=gb, h=hb, o=q1: q.scalar_tensor_tensor(o, g, 0.5, h, Op.mult, Op.add))
            op("dv", lambda q, a=q1, r=sc, n=nb, o=q2: q.tensor_tensor(
                o.rearrange("p (b i) -> p b i", b=n),
                a.rearrange("p (b i) -> p b i", b=n),
                r[:, :, None].broadcast_to([128, n, 16]), Op.mult))
            op("dv", lambda q, s=src, m=q2, o=q1: q.scalar_tensor_tensor(o, s, 0.0, m, Op.is_gt, Op.mult))
            op("dv", lambda q, a=q1, m=q2, o=dst: q.scalar_tensor_tensor(o, a, 2.0, m, Op.mult, Op.subtract),
               sem="dv")

        def emit_ln(rt, dst):
            wait("dv", "ac")
            op("dv", lambda q, r=rt: q.tensor_reduce(lsum, r, AX.X, Op.add), sem="dv")
            wait("dv", "dv")
            op("dv", lambda q: q.tensor_scalar(lmean, lsum, 1.0 / D, None, Op.mult), sem="dv")
            wait("dv", "dv")
            op("dv", lambda q, r=rt: q.tensor_scalar(xcb[:], r, lmean, None, Op.subtract), sem="dv")
            wait("ac", "dv")
            op("ac", lambda q: q.activation(sqdump[:], xcb[:], AF.Square, accum_out=lvar), sem="ac")
            wait("ac", "ac")
            op("ac", lambda q: q.activation(llnt, lvar, AF.Ln, bias=eps1, scale=1.0 / D), sem="ac")
            wait("ac", "ac")
            op("ac", lambda q: q.activation(lrstd, llnt, AF.Exp, scale=-0.5), sem="ac")
            wait("dv", "ac")
            op("dv", lambda q, o=dst: q.tensor_scalar(o, xcb[:], lrstd, None, Op.mult))

        for i in range(4):
            dma("sync", R[:, i * 4096:(i + 1) * 4096].rearrange("p (t d) -> p t d", d=1024),
                x_d.rearrange("(t p) d -> p t d", p=128)[:, i * 4:(i + 1) * 4, :], "ld")
        op("dv", lambda q: q.memset(eps1, 1e-5))
        op("dv", lambda q: q.memset(ones1[:], 1.0))
        op("dv", lambda q: q.memset(ones1f[:], 1.0))
        op("gpsimd", lambda q: q.iota(iotf[:], [[1, 896]], base=-384, channel_multiplier=-1,
                                      allow_small_or_imprecise_dtypes=True), sem="gld", k=1)
        wait("dv", "gld")
        op("dv", lambda q: q.tensor_scalar(mask[:], iotf[:], 0.0, None, Op.is_ge), sem="dv")
        op("gpsimd", lambda q: q.iota(identf[:], [[1, 128]], base=0, channel_multiplier=-1,
                                      allow_small_or_imprecise_dtypes=True), sem="gld", k=1)
        wait("dv", "gld")
        op("dv", lambda q: q.tensor_scalar(identf[:], identf[:], 0.0, None, Op.is_equal), sem="dv")
        for t, d in ((bqt, bq_d), (bkt, bk_d)):
            for ntile in range(2):
                dma("sync", t[:, ntile:ntile + 1],
                    d.rearrange("o (n p) -> p (o n)", p=128)[:, ntile:ntile + 1], "ld")
        s_, _ = bump("gld", 16)
        emit("gpsimd", lambda q, s=s_: q.dma_start(out=bvr[:], in_=bv_d[:]).then_inc(s, 16))
        dma("sync", xcb[0:1, :], bo_d[:], "ld")
        wait("dv", "ld")
        op("dv", lambda q: q.tensor_scalar(boq[:], xcb[0:1, :], 0.25, None, Op.mult), sem="dv")

        jobs = []
        for (srcw, dstw, rowsw, widthw) in ((wg_d, wgq_s, I, D), (wu_d, wuq_s, I, D),
                                            (wd_d, wdq_s, D, I)):
            for t in range(rowsw // 128):
                jobs.append((srcw, dstw, t, widthw))
        ldm = {}
        dvm = {}
        stm = {}
        for k, (srcw, dstw, t, width) in enumerate(jobs):
            par = k % 2
            wb = (wstage2 if par else wstage)[:, 0:width]
            ob = qout2 if par else qout
            if par in dvm:
                wait("sync", "dv", dvm[par])
            dma("sync", wb, srcw[t * 128:(t + 1) * 128, :], "ld")
            ldm[par] = cnt["ld"]
            wait("dv", "ld", ldm[par])
            if par in stm:
                wait("dv", "st", stm[par])
            for o in range(0, width, 1024):
                sw = min(1024, width - o)
                if o > 0:
                    wait("dv", "st")
                quant_chain(wb[:, o:o + sw], sw, ob[:, 0:sw])
                wait("sync", "dv")
                dma("sync", dstw[t * 128:(t + 1) * 128, o:o + sw], ob[:, 0:sw], "st")
            dvm[par] = cnt["dv"]
            stm[par] = cnt["st"]
        wq_done_st = cnt["st"]

        def emit_ffn():
            for i in range(NT):
                rt = R[:, i * 1024:(i + 1) * 1024]
                emit_ln(rt, xnf[:])
                wait("dv", "st")
                quant_chain(xnf[:], 1024, qout)
                wait("sync", "dv")
                dma("sync", xq_s[i * 128:(i + 1) * 128, :], qout, "st")
            wait("sync", "st")
            for kc in range(8):
                dma("sync", xqT[:, kc * 2048:(kc + 1) * 2048],
                    xq_s[:, kc * 128:(kc + 1) * 128], "ld", transpose=True)

            slabs = [(0, 512), (512, 512), (1024, 512), (1536, 512), (2048, 512), (2560, 256)]
            wld_marks = []

            def load_wslab(s):
                i0, w = slabs[s]
                par = s % 2
                wait("sync", "te")
                for kc in range(8):
                    dma("sync", wgT[par][:, kc * 512:kc * 512 + w],
                        wgq_s[i0:i0 + w, kc * 128:(kc + 1) * 128], "ld", transpose=True)
                    dma("sync", wuT[par][:, kc * 512:kc * 512 + w],
                        wuq_s[i0:i0 + w, kc * 128:(kc + 1) * 128], "ld", transpose=True)
                wld_marks.append(cnt["ld"])

            load_wslab(0)
            load_wslab(1)
            it = 0
            free_g = {}
            free_u = {}
            hq_st = {}
            for s in range(6):
                i0, w = slabs[s]
                par = s % 2
                wait("te", "ld", wld_marks[s])
                for tt in range(NT):
                    p = it % 2
                    gbank = bank(0 + p * 2)
                    ubank = bank(1 + p * 2)
                    if p in free_g:
                        wait("te", "ac", free_g[p])
                    if p in free_u:
                        wait("te", "dv", free_u[p])
                    for kc in range(8):
                        op("te", lambda q, o=gbank[:, 0:w],
                           l=xqT[:, kc * 2048 + tt * 128: kc * 2048 + tt * 128 + 128],
                           r=wgT[par][:, kc * 512:kc * 512 + w], st=(kc == 0), sp=(kc == 7):
                           q.matmul(o, l, r, start=st, stop=sp),
                           sem="te" if kc == 7 else None)
                    g_done = cnt["te"]
                    for kc in range(8):
                        op("te", lambda q, o=ubank[:, 0:w],
                           l=xqT[:, kc * 2048 + tt * 128: kc * 2048 + tt * 128 + 128],
                           r=wuT[par][:, kc * 512:kc * 512 + w], st=(kc == 0), sp=(kc == 7):
                           q.matmul(o, l, r, start=st, stop=sp),
                           sem="te" if kc == 7 else None)
                    u_done = cnt["te"]
                    wait("ac", "te", u_done if SILU_U else g_done)
                    wait("ac", "dv", free_u.get(p, 0))
                    op("ac", lambda q, i_=gbank[:, 0:w], o=gbuf[p][:, 0:w]:
                       q.activation(o, i_, AF.Silu), sem="ac")
                    free_g[p] = cnt["ac"]
                    wait("dv", "ac")
                    wait("dv", "te", u_done)
                    wait("dv", "st", hq_st.get(p, 0))
                    op("dv", lambda q, a=gbuf[p][:, 0:w], b=ubank[:, 0:w], o=hbuf[p][:, 0:w]:
                       q.tensor_tensor(o, a, b, Op.mult), sem="dv")
                    free_u[p] = cnt["dv"]
                    wait("dv", "st")
                    quant_chain(hbuf[p][:, 0:w], w, hqtile[p][:, 0:w])
                    wait("sync", "dv")
                    dma("sync", hq_s[tt * 128:(tt + 1) * 128, i0:i0 + w], hqtile[p][:, 0:w], "st")
                    hq_st[p] = cnt["st"]
                    it += 1
                if s + 2 < 6:
                    load_wslab(s + 2)
            wait("sync", "st")
            wait("te", "ac")
            wait("te", "dv")
            free_d = {}
            for th in range(2):
                for ic in range(22):
                    dma("sync", hqT[:, ic * 1024:(ic + 1) * 1024],
                        hq_s[th * 1024:(th + 1) * 1024, ic * 128:(ic + 1) * 128],
                        "ld", transpose=True)
                hq_ld = cnt["ld"]
                for nh in range(2):
                    for ic in range(22):
                        dma("sync", wdT[:, ic * 512:(ic + 1) * 512],
                            wdq_s[nh * 512:(nh + 1) * 512, ic * 128:(ic + 1) * 128],
                            "ld", transpose=True)
                    wd_ld = cnt["ld"]
                    wait("te", "ld", hq_ld)
                    wait("te", "ld", wd_ld)
                    for t8 in range(8):
                        tt = th * 8 + t8
                        p = t8 % 2
                        dbank = bank(p)
                        if p in free_d:
                            wait("te", "dv", free_d[p])
                        for ic in range(22):
                            op("te", lambda q, o=dbank,
                               l=hqT[:, ic * 1024 + t8 * 128: ic * 1024 + t8 * 128 + 128],
                               r=wdT[:, ic * 512:(ic + 1) * 512], st=(ic == 0), sp=(ic == 21):
                               q.matmul(o, l, r, start=st, stop=sp),
                               sem="te" if ic == 21 else None)
                        wait("dv", "te", cnt["te"] + (1 if FENCE_LVL >= 1 else 0))
                        rsl = R[:, tt * 1024 + nh * 512: tt * 1024 + nh * 512 + 512]
                        op("dv", lambda q, b=dbank, r=rsl:
                           q.scalar_tensor_tensor(r, b, 0.5, r, Op.mult, Op.add), sem="dv")
                        free_d[p] = cnt["dv"]
                    wait("sync", "te")
                wait("sync", "te")
            if FENCE_LVL >= 1:
                te_fence(bank(2))

        emit_ffn()

        if N_PHASES >= 2:
            for i in range(NT):
                rt = R[:, i * 1024:(i + 1) * 1024]
                wait("dv", "st")
                emit_ln(rt, xnbf[:])
                op("dv", lambda q: q.tensor_copy(sf[:, 10:11], xnbf[:, 0:1]), sem="dv")
                wait("sync", "dv")
                dma("sync", ag1_in[i * 128:(i + 1) * 128, :], xnbf[:], "st")
            ag_ready = cnt["st"]
            wait("gpsimd", "te")
            wait("gpsimd", "dv")
            for wsb, wdr in ((wq_sb, wq_d), (wk_sb, wk_d), (wv_sb, wv_d)):
                s_, _ = bump("gld", 16)
                emit("gpsimd", lambda q, w=wsb, dd=wdr, s=s_: q.dma_start(
                    out=w.rearrange("p (kc n) -> p kc n", n=256),
                    in_=dd.rearrange("(kc p) n -> p kc n", p=128)).then_inc(s, 16))
            wait("gpsimd", "st", ag_ready)
            s_, _ = bump("cc1", 1)
            emit("gpsimd", lambda q, s=s_: q.collective_compute(
                "AllGather", mybir.AluOpType.bypass, replica_groups=groups,
                ins=[ag1_in.opt()], outs=[ag1_out.opt()]).then_inc(s, 1))

            wait("sync", "cc1", 1)
            wait("te", "gld")
            x_marks = []

            def load_xslice(ts):
                par = ts % 2
                wait("sync", "te")
                for kc in range(8):
                    dma("sync", xn2Tb[par][:, kc * 512:(kc + 1) * 512],
                        ag1_out[ts * 512:(ts + 1) * 512, kc * 128:(kc + 1) * 128],
                        "ld", transpose=True)
                x_marks.append(cnt["ld"])

            load_xslice(0)
            load_xslice(1)
            pj_free = {}
            st_stage = 0
            for ts in range(16):
                par = ts % 2
                wait("te", "ld", x_marks[ts])
                for pi, (wsb, dst_s, bt) in enumerate(
                        ((wq_sb, qt_s, bqt), (wk_sb, kt_s, bkt))):
                    for ntile in range(2):
                        bi = (pi * 2 + ntile) % 4
                        pb = bank(bi)
                        if bi in pj_free:
                            wait("te", "dv", pj_free[bi])
                        for kc in range(8):
                            op("te", lambda q, o=pb,
                               l=wsb[:, kc * 256 + ntile * 128: kc * 256 + ntile * 128 + 128],
                               r=xn2Tb[par][:, kc * 512:(kc + 1) * 512], st=(kc == 0), sp=(kc == 7):
                               q.matmul(o, l, r, start=st, stop=sp),
                               sem="te" if kc == 7 else None)
                        wait("dv", "te", cnt["te"] + 1)
                        wait("dv", "st", st_stage)
                        op("dv", lambda q, i_=pb, b=bt[:, ntile:ntile + 1]:
                           q.tensor_scalar(pjstage[:], i_, b, None, Op.add), sem="dv")
                        pj_free[bi] = cnt["dv"]
                        wait("sync", "dv")
                        dma("sync", dst_s[ntile, :, ts * 512:(ts + 1) * 512], pjstage[:], "st")
                        st_stage = cnt["st"]
                for vt in range(4):
                    bi = 4 + (vt % 2)
                    pb = bank(bi)
                    if bi in pj_free:
                        wait("te", "dv", pj_free[bi])
                    for kc in range(8):
                        op("te", lambda q, o=pb[:, 0:256],
                           l=xn2Tb[par][:, kc * 512 + vt * 128: kc * 512 + vt * 128 + 128],
                           r=wv_sb[:, kc * 256:(kc + 1) * 256], st=(kc == 0):
                           q.matmul(o, l, r, start=st, stop=False))
                    op("te", lambda q, o=pb[:, 0:256]: q.matmul(
                        o, ones1[:], bvr[:], start=False, stop=True), sem="te")
                    wait("dv", "te", cnt["te"] + 1)
                    wait("dv", "st", st_stage)
                    op("dv", lambda q, i_=pb[:, 0:256]:
                       q.tensor_copy(pjstage[:, 0:256], i_), sem="dv")
                    pj_free[bi] = cnt["dv"]
                    wait("sync", "dv")
                    dma("sync", v_s[(ts * 4 + vt) * 128:(ts * 4 + vt) * 128 + 128, :],
                        pjstage[:, 0:256], "st")
                    st_stage = cnt["st"]
                if ts + 2 < 16:
                    load_xslice(ts + 2)
            te_fence(bank(6))
            proj_done_st = cnt["st"]

            p_t = [PS[:, 0:1024], PS[:, 1024:2048], PS[:, 2048:3072]]
            avb = [bank(6), bank(7)]
            par_free = {}
            ptr_free = {}

            def load_ktq(idx):
                hp, g = divmod(idx, 16)
                par = idx % 2
                wait("sync", "te")
                wait("sync", "st")
                n = 512 * (g + 1)
                dma("sync", ktb[par][:, 0:n], kt_s[hp, :, 0:n], "ld")
                dma("sync", qtb[par][:, :], qt_s[hp, :, g * 512:(g + 1) * 512], "ld")
                return cnt["ld"]

            wait("sync", "st", proj_done_st)
            wait("sync", "te")
            for h2 in range(2):
                dma("sync", VAb[0].rearrange("p (h t o) -> p h t o", h=2, t=64)[:, h2, :, 0:64],
                    v_s.rearrange("(t p) (hp h d) -> p hp h t d", p=128, hp=2, h=2)[:, 0, h2, :, :],
                    "ld")
            op("dv", lambda q: q.memset(
                VAb[0].rearrange("p (h t o) -> p h t o", h=2, t=64)[:, :, :, 64:65], 1.0),
               sem="dv")
            va_ready = [cnt["dv"], None]
            ktq_marks = [load_ktq(0), load_ktq(1)]
            wait("te", "dv")
            for hp in range(2):
                if hp == 0:
                    for h2 in range(2):
                        dma("sync", VAb[1].rearrange("p (h t o) -> p h t o", h=2, t=64)[:, h2, :, 0:64],
                            v_s.rearrange("(t p) (hp h d) -> p hp h t d", p=128, hp=2, h=2)[:, 1, h2, :, :],
                            "ld")
                    op("dv", lambda q: q.memset(
                        VAb[1].rearrange("p (h t o) -> p h t o", h=2, t=64)[:, :, :, 64:65], 1.0),
                       sem="dv")
                    va_ready[1] = cnt["dv"]
                for g in range(16):
                    idx = hp * 16 + g
                    kpar = idx % 2
                    nsc = 4 * g + 4
                    order = list(range(4 * g, 4 * g + 4)) + list(range(0, 4 * g))
                    wait("te", "ld", ktq_marks[idx])
                    wait("te", "dv", va_ready[hp])
                    mc_done = None
                    dance_dv = None
                    st_marks = []

                    def emit_pair(si):
                        kc = order[si]
                        par = si % 3
                        pt = p_t[par]
                        if par in par_free:
                            wait("te", "ac", par_free[par])
                        if si == 3 and dance_dv is not None:
                            wait("te", "dv", dance_dv)
                        op("te", lambda q, o=pt[:, 0:512], l=ktb[kpar][0:64, kc * 128:(kc + 1) * 128],
                           r=qtb[kpar][0:64, :]:
                           q.matmul(o, l, r, start=True, stop=True))
                        op("te", lambda q, o=pt[:, 512:1024], l=ktb[kpar][64:128, kc * 128:(kc + 1) * 128],
                           r=qtb[kpar][64:128, :]:
                           q.matmul(o, l, r, start=True, stop=True), sem="te")
                        st_marks.append(cnt["te"])

                    def emit_consumer(si, fence_val):
                        kc = order[si]
                        par = si % 3
                        slot = si % 3
                        pt = p_t[par]
                        w = fence_val
                        if slot in ptr_free:
                            w = max(w, ptr_free[slot])
                        wait("ac", "te", w)
                        if si == 0:
                            wait("ac", "dv", mc_done)
                        op("ac", lambda q, i_=pt, o=PTr[slot]: q.activation(
                            o, i_, AF.Exp, bias=mcol, scale=0.125), sem="ac")
                        par_free[par] = cnt["ac"]
                        exp_done = cnt["ac"]
                        gate_sem, gate_val = "ac", exp_done
                        if kc >= 4 * g:
                            moff = 384 - 128 * (kc - 4 * g)
                            wait("dv", "ac", exp_done)
                            op("dv", lambda q, o=PTr[slot], mo=moff: q.tensor_tensor(
                                o.rearrange("p (e f) -> p e f", e=2),
                                o.rearrange("p (e f) -> p e f", e=2),
                                mask[:, mo:mo + 512][:, None, :].broadcast_to([128, 2, 512]),
                                Op.mult), sem="dv")
                            gate_sem, gate_val = "dv", cnt["dv"]
                        if DEBUG_DUMPS and hp == 0 and g == 1:
                            wait("sync", gate_sem, gate_val)
                            dma("sync", pt_dbg[si, :, :], PTr[slot][:], "st")
                            gate_sem, gate_val = "st", cnt["st"]
                        wait("te", gate_sem, gate_val)
                        for e in range(2):
                            op("te", lambda q, o=avb[e][0:65, :],
                               l=VAb[hp][:, e * 4160 + kc * 65: e * 4160 + kc * 65 + 65],
                               r=PTr[slot][:, e * 512:(e + 1) * 512],
                               st=(si == 0), sp=(si == nsc - 1):
                               q.matmul(o, l, r, start=st, stop=sp),
                               sem="te" if e == 1 else None)
                        ptr_free[slot] = cnt["te"]

                    for si in range(nsc):
                        emit_pair(si)
                        if si == 1:
                            avB0 = avb[0]
                            avB1 = avb[1]
                            wait("te", "ac")
                            wait("te", "dv")
                            wait("dv", "te", st_marks[1])
                            wait("dv", "ac")
                            op("dv", lambda q: q.tensor_reduce(
                                cmcol, p_t[0], AX.X, Op.max), sem="dv")
                            dance_dv = cnt["dv"]
                            wait("te", "dv", dance_dv)
                            op("te", lambda q, o=avB0: q.transpose(
                                o[0:1, 0:128], cmcol, identf[:]), sem="te")
                            f1 = te_fence(avB1)
                            wait("dv", "te", f1)
                            op("dv", lambda q, a=avB0: q.tensor_reduce(
                                msb, a[0:1, 0:128], AX.X, Op.max), sem="dv")
                            wait("dv", "dv")
                            op("dv", lambda q: q.tensor_copy(mbf[:], msb), sem="dv")
                            wait("te", "dv")
                            op("te", lambda q, o=avB0: q.matmul(
                                o[:, 0:1], ones1[:], mbf[:], start=True, stop=True), sem="te")
                            f2 = te_fence(avB1)
                            wait("dv", "te", f2)
                            op("dv", lambda q, a=avB0: q.tensor_scalar(
                                mcol, a[:, 0:1], -0.125, None, Op.mult), sem="dv")
                            mc_done = cnt["dv"]
                            if DEBUG_DUMPS:
                                wait("sync", "dv", mc_done)
                                dma("sync", mc_dbg.rearrange("h g p -> (h g) p")[
                                    hp * 16 + g: hp * 16 + g + 1, :].rearrange("o p -> p o"),
                                    mcol, "st")
                            emit_consumer(0, st_marks[1])
                        elif si >= 2:
                            emit_consumer(si - 1, st_marks[si])
                    wait("te", "ac", par_free.get(nsc % 3, 0))
                    f3 = te_fence(p_t[nsc % 3][:, 0:512])
                    emit_consumer(nsc - 1, f3)
                    wait("te", "ac", par_free.get((nsc + 1) % 3, 0))
                    f4 = te_fence(p_t[(nsc + 1) % 3][:, 0:512])
                    Pg = p_t[(nsc + 1) % 3]
                    for e in range(2):
                        ab = avb[e]
                        wait("ac", "te", f4)
                        op("ac", lambda q, a=ab: q.activation(rln[:], a[64:65, :], AF.Ln), sem="ac")
                        wait("te", "ac")
                        op("te", lambda q, sp=Pg: q.matmul(
                            sp[0:64, 0:512], ones1f[0:1, 0:64], rln[:], start=True, stop=True),
                           sem="te")
                        f5 = te_fence(Pg[:, 512:1024])
                        wait("ac", "te", f5)
                        wait("ac", "dv")
                        op("ac", lambda q, sp=Pg: q.activation(
                            rlb[:], sp[0:64, 0:512], AF.Exp, scale=-1.0), sem="ac")
                        if DEBUG_DUMPS:
                            wait("sync", "ac")
                            dma("sync", rlb_dbg.rearrange("h g v -> (h g) v")[(2 * hp + e) * 16 + g:(2 * hp + e) * 16 + g + 1, :], rlb[0:1, :], "st")
                        wait("dv", "ac")
                        wait("dv", "st")
                        op("dv", lambda q, a=ab: q.tensor_tensor(
                            oT_st[:], a[0:64, :], rlb[:], Op.mult), sem="dv")
                        wait("sync", "dv")
                        dma("sync", attnT_s[hp, e * 64:(e + 1) * 64, g * 512:(g + 1) * 512],
                            oT_st[:], "st")
                    par_free[(nsc + 1) % 3] = cnt["ac"]
                    if idx + 2 < 32:
                        ktq_marks.append(load_ktq(idx + 2))

            wait("gpsimd", "te")
            s_, _ = bump("gld", 16)
            emit("gpsimd", lambda q, s=s_: q.dma_start(
                out=wo_sb.rearrange("p (kc n) -> p kc n", n=1024),
                in_=wo_d.rearrange("(kc p) n -> p kc n", p=128)).then_inc(s, 16))
            wait("te", "gld")
            wait("te", "ac")
            wait("te", "dv")
            lw_marks = []

            def load_lhs(grp):
                par = grp % 2
                wait("sync", "te")
                for hp in range(2):
                    dma("sync", lhsT_pw[par * 2 + hp],
                        attnT_s[hp, :, grp * 1024:(grp + 1) * 1024], "ld")
                lw_marks.append(cnt["ld"])

            wait("sync", "st")
            load_lhs(0)
            load_lhs(1)
            pw_free = {}
            rsb_st = {}
            for grp in range(8):
                par = grp % 2
                wait("te", "ld", lw_marks[grp])
                for sub in range(8):
                    p = (grp * 8 + sub) % 2
                    pb2 = PS[:, p * 1024:(p + 1) * 1024]
                    if p in pw_free:
                        wait("te", "dv", pw_free[p])
                    for nhf in range(2):
                        o = pb2[:, nhf * 512:(nhf + 1) * 512]
                        for hp in range(2):
                            op("te", lambda q, o=o,
                               l=lhsT_pw[par * 2 + hp][:, sub * 128:(sub + 1) * 128],
                               r=wo_sb[:, hp * 1024 + nhf * 512: hp * 1024 + nhf * 512 + 512],
                               st=(hp == 0):
                               q.matmul(o, l, r, start=st, stop=False))
                        op("te", lambda q, o=o, r=boq[:, nhf * 512:(nhf + 1) * 512]:
                           q.matmul(o, ones1[:], r, start=False, stop=True), sem="te")
                    wait("dv", "te", cnt["te"] + 1)
                    wait("dv", "st", rsb_st.get(p, 0))
                    op("dv", lambda q, i_=pb2, o=rsbuf[p]: q.tensor_copy(o, i_), sem="dv")
                    pw_free[p] = cnt["dv"]
                    wait("sync", "dv")
                    tt = grp * 8 + sub
                    dma("sync", rs_in[tt * 128:(tt + 1) * 128, :], rsbuf[p], "st")
                    rsb_st[p] = cnt["st"]
                if grp + 2 < 8:
                    load_lhs(grp + 2)
            te_fence(bank(4))
            wait("gpsimd", "st")
            s_, _ = bump("cc2", 1)
            emit("gpsimd", lambda q, s=s_: q.collective_compute(
                "ReduceScatter", mybir.AluOpType.add, replica_groups=groups,
                ins=[rs_in.opt()], outs=[rs_out.opt()]).then_inc(s, 1))
            wait("sync", "cc2", 1)
            rsd = {}
            for i in range(NT):
                par = i % 2
                if par in rsd:
                    wait("sync", "dv", rsd[par])
                dma("sync", rsbuf[par], rs_out[i * 128:(i + 1) * 128, :], "ld")
                wait("dv", "ld")
                op("dv", lambda q, b=rsbuf[par], r=R[:, i * 1024:(i + 1) * 1024]:
                   q.scalar_tensor_tensor(r, b, 1.0, r, Op.mult, Op.add), sem="dv")
                rsd[par] = cnt["dv"]

        if N_PHASES >= 3:
            emit_ffn()

        if FENCE_LVL >= 2:
            te_fence(bank(4))
        if DEBUG_DUMPS and N_PHASES >= 2:
            dma("sync", ag1_dbg[:], ag1_out[:], "st")
            dma("sync", rso_dbg[:], rs_out[:], "st")
            dma("sync", rsi_dbg[:], rs_in[:], "st")
        wait("sync", "dv")
        for i in range(4):
            dma("sync", out_d.rearrange("(t p) d -> p t d", p=128)[:, i * 4:(i + 1) * 4, :],
                R[:, i * 4096:(i + 1) * 4096].rearrange("p (t d) -> p t d", d=1024), "st")
        wait("sync", "st")

        with nc.Block() as block:
            @block.sync
            def _(q):
                for e, fn in plan:
                    if e == "sync":
                        fn(q)

            @block.tensor
            def _(q):
                for e, fn in plan:
                    if e == "te":
                        fn(q)

            @block.vector
            def _(q):
                for e, fn in plan:
                    if e == "dv":
                        fn(q)

            @block.scalar
            def _(q):
                for e, fn in plan:
                    if e == "ac":
                        fn(q)

            @block.gpsimd
            def _(q):
                for e, fn in plan:
                    if e == "gpsimd":
                        fn(q)

    return nc


_NC_CACHE = None


def kernel(**inputs):
    global _NC_CACHE
    X = np.ascontiguousarray(np.asarray(inputs["L0_attn_inputs"], np.float32).reshape(16384, 1024))
    Wq = np.asarray(inputs["L0_attn_Wq"], np.float32)
    Wk = np.asarray(inputs["L0_attn_Wk"], np.float32)
    Wv = np.asarray(inputs["L0_attn_Wv"], np.float32)
    Wo = np.asarray(inputs["L0_attn_Wo"], np.float32)
    bq = np.asarray(inputs["L0_attn_bq"], np.float32)
    bk = np.asarray(inputs["L0_attn_bk"], np.float32)
    bv = np.asarray(inputs["L0_attn_bv"], np.float32)
    bo = np.asarray(inputs["L0_attn_bo"], np.float32)
    wg = np.ascontiguousarray(np.asarray(inputs["L0_ffn_w_gate"], np.float32))
    wu = np.ascontiguousarray(np.asarray(inputs["L0_ffn_w_up"], np.float32))
    wd = np.ascontiguousarray(np.asarray(inputs["L0_ffn_w_down"], np.float32))

    if _NC_CACHE is None:
        _NC_CACHE = build_nc()
    nc = _NC_CACHE

    in_maps = []
    for c in range(8):
        s = 256 * (c % 4)
        in_maps.append({
            "x": np.ascontiguousarray(X[c * TOK:(c + 1) * TOK]),
            "wg": wg, "wu": wu, "wd": wd,
            "wqo": np.ascontiguousarray(Wq[:, s:s + 256]),
            "wko": np.ascontiguousarray(Wk[:, s:s + 256]),
            "wvo": np.ascontiguousarray(Wv[:, s:s + 256]),
            "woo": np.ascontiguousarray(Wo[s:s + 256, :]),
            "bqo": np.ascontiguousarray(bq[s:s + 256].reshape(1, 256)),
            "bko": np.ascontiguousarray(bk[s:s + 256].reshape(1, 256)),
            "bvo": np.ascontiguousarray(bv[s:s + 256].reshape(1, 256)),
            "bo": np.ascontiguousarray(bo.reshape(1, 1024)),
        })
    res = run_bass_kernel_spmd(nc, in_maps, core_ids=list(range(8)))
    if DEBUG_DUMPS:
        global LAST_RES
        LAST_RES = res.results
    out = np.concatenate([np.asarray(res.results[c]["out"]) for c in range(8)], axis=0)
    return out.reshape(2, 8192, 1024)
